# version 05 (final)
# baseline (speedup 1.0000x reference)
import numpy as np
import ml_dtypes

import concourse.bass as bass
import concourse.mybir as mybir
from concourse.bass_utils import run_bass_kernel_spmd

F32 = mybir.dt.float32
BF16 = mybir.dt.bfloat16
AF = mybir.ActivationFunctionType
ALU = mybir.AluOpType
AXX = mybir.AxisListType.X

B, S, D = 4, 4096, 1024
H, KVH, HD, LAT, WIN = 16, 4, 64, 256, 256
E, FHD = 16, 128
T = 2048
NG = 4
GW = 512
NC = 8
EPS_RMS = 1e-6
EPS_LN = 1e-5
NCORES = 8

PAIR = [(0, 4), (1, 5), (2, 6), (3, 7), (8, 12), (9, 13), (10, 14), (11, 15)]
SLOT_HEAD = [PAIR[s // 2][s % 2] for s in range(16)]
SLOT_KV = [h // 4 for h in SLOT_HEAD]
GROUPS = [[0, 1], [2, 3], [4, 5], [6, 7]]
CBIAS_VALS = [0.0, 1.0, 1e-5, 1e-6, -8.0, 1e-12, 1e-20, 2.0]


def _deinterleave_cols(w, head_w):
    w = np.asarray(w)
    n = w.shape[-1]
    idx = []
    for h0 in range(0, n, head_w):
        idx += list(range(h0, h0 + head_w, 2)) + list(range(h0 + 1, h0 + head_w, 2))
    return w[..., idx]


def _rope_tables(positions, base, rows=128, blk=64):
    inv = 1.0 / base ** (np.arange(0, blk, 2, dtype=np.float64) / blk)
    fr = positions[None, :].astype(np.float64) * inv[:, None]
    cos, sin = np.cos(fr), np.sin(fr)
    Cm = np.zeros((rows, len(positions)), np.float32)
    Sm = np.zeros((rows, len(positions)), np.float32)
    for r in range(rows):
        i = r % 32
        Cm[r] = cos[i]
        Sm[r] = sin[i] if (r % 64) >= 32 else -sin[i]
    return Cm, Sm


ENGS = ["t", "v", "s", "y", "g"]
DMA_SEMS = ["yc", "yt", "yw0", "yw1", "yw2", "yw3", "yw4", "yw5", "yd0", "yd1", "ye", "gb", "gs", "go", "gr"]
SEMS = ["t", "v", "s", "p", "c"] + DMA_SEMS
ENG_OF_SEM = {"t": "t", "v": "v", "s": "s", "p": "g", "c": "g",
              "yc": "y", "yt": "y", "yw0": "y", "yw1": "y", "yw2": "y",
              "yw3": "y", "yw4": "y", "yw5": "y", "yd0": "y",
              "yd1": "y", "ye": "y", "gb": "g", "gs": "g", "go": "g", "gr": "g"}


class Plan:
    def __init__(self):
        self.items = {e: [] for e in ENGS}
        self.cnt = {s: 0 for s in SEMS}
        self.waited = {e: {} for e in ENGS}
        self.pend = {e: [] for e in ENGS}

    def _emit(self, ek, build, deps, sem, inc):
        deps = list(self.pend[ek]) + list(deps)
        self.pend[ek] = []
        waits = []
        seen = {}
        for d in deps:
            if d is None:
                continue
            ps, val = d
            if val <= 0:
                continue
            seen[ps] = max(seen.get(ps, 0), val)
        for ps, val in seen.items():
            if self.waited[ek].get(ps, 0) < val:
                waits.append((ps, val))
                self.waited[ek][ps] = val
        self.cnt[sem] += inc
        self.items[ek].append((waits, build, sem, inc))
        return (sem, self.cnt[sem])

    def op(self, ek, build, deps=()):
        sem = "p" if ek == "g" else ek
        return self._emit(ek, build, deps, sem, 1)

    def dma(self, ek, build, deps=(), sem=None):
        if sem is None:
            sem = "yc" if ek == "y" else "gb"
        return self._emit(ek, build, deps, sem, 16)

    def cc(self, build, deps=()):
        return self._emit("g", build, deps, "c", 1)

    def barrier(self):
        toks = [(s, self.cnt[s]) for s in SEMS if self.cnt[s] > 0]
        for e in ENGS:
            self.pend[e] = list(toks)

    def fork(self):
        import copy
        return (copy.deepcopy(self.cnt), copy.deepcopy(self.waited),
                copy.deepcopy(self.pend), {e: len(self.items[e]) for e in ENGS})

    def restore(self, st):
        import copy
        cnt, waited, pend, lens = st
        self.cnt = copy.deepcopy(cnt)
        self.waited = copy.deepcopy(waited)
        self.pend = copy.deepcopy(pend)
        out = {e: self.items[e][lens[e]:] for e in ENGS}
        for e in ENGS:
            del self.items[e][lens[e]:]
        return out

    def divergent(self, fn):
        st = self.fork()
        fn(0)
        cnt0 = dict(self.cnt)
        body0 = self.restore(st)
        fn(1)
        cnt1 = dict(self.cnt)
        body1 = self.restore(st)
        merged = {s: max(cnt0[s], cnt1[s]) for s in SEMS}
        for body, cnt in ((body0, cnt0), (body1, cnt1)):
            for s in SEMS:
                d = merged[s] - cnt[s]
                if d > 0:
                    body[ENG_OF_SEM[s]].append(([], ("PAD", s, d, cnt[s]), None, 0))
        self.cnt = merged
        self.waited = {e: {} for e in ENGS}
        for e in ENGS:
            if body0[e] or body1[e]:
                self.items[e].append(("SWITCH", body0[e], body1[e]))


class Builder:
    QT_O = 0
    XN_O = 16384
    KST_O = 20480
    VST_O = 21504
    LT_O = 22544
    WSM_O = 23568
    WQF_O = 27664
    WOF_O = 16384
    KVR_O = 18432
    VG_O = 26624
    VLAST_O = 34944
    QF_O = 0
    KFT_O = 2048
    VFT_O = 4096
    QTK_O = 6144
    RBW_O = 8192
    KF_O = 8192
    VGF_O = 14336
    VLASTF_O = 18432
    AOT_O = 2048
    AOE_O = 4096
    WOE_O = 12544
    VB_W = 35984

    ACC_O = 0
    XD_O = 1536
    LGT_O = 2560
    QKN_O = 3072
    VF_W = 4096

    def __init__(self):
        nc = bass.Bass("TRN2", target_bir_lowering=False)
        self.nc = nc
        self.P = Plan()
        self._wrr = 0
        self._wslot_tok = {}
        self._mk_io()
        self._mk_sbuf()
        self._mk_psum()
        for i, v in enumerate(CBIAS_VALS):
            nc.const_aps.register(F32, v, self.c_cbias[:, i:i + 1])

    def _mk_io(self):
        nc = self.nc
        dr = lambda n, sh, dt=F32, k="ExternalInput": nc.dram_tensor(n, sh, dt, kind=k).ap()
        self.x = dr("x", [D, T])
        self.wq_g = dr("wq_g", [D, D])
        self.kvd = dr("kvd", [D, LAT])
        self.kup = dr("kup", [LAT, LAT])
        self.vup = dr("vup", [LAT, LAT])
        self.wo_g = dr("wo_g", [D, D])
        self.wq_l = dr("wq_l", [D, D])
        self.wk_l = dr("wk_l", [D, LAT])
        self.wv_l = dr("wv_l", [D, LAT])
        self.wo_l = dr("wo_l", [D, D])
        self.sim = dr("sim", [D, E])
        self.gates = dr("gates", [1, E])
        self.qkvf = dr("qkvf", [E, D, 3 * FHD])
        self.oproj = dr("oproj", [E, FHD, D])
        self.n1w = dr("n1w", [128, 8])
        self.n2w = dr("n2w", [128, 8])
        self.ctab_g = dr("ctab_g", [128, T])
        self.stab_g = dr("stab_g", [128, T])
        self.ctab_l = dr("ctab_l", [128, T])
        self.stab_l = dr("stab_l", [128, T])
        self.identb = dr("identb", [128, 128], BF16)
        self.identf = dr("identf", [128, 128])
        self.pswap = dr("pswap", [128, 128], BF16)
        self.trailm = dr("trailm", [128, 8 * GW], BF16)
        self.trilm = dr("trilm", [128, 128], BF16)
        self.bandm = dr("bandm", [128, 128], BF16)
        self.sel2 = dr("sel2", [16, 8 * 128])
        self.selk = dr("selk", [4, 2 * 128])
        self.sel4 = dr("sel4", [4, 4 * 128])
        self.tcsel = dr("tcsel", [128, 16])
        self.hsel = dr("hsel", [128, 8 * 16])
        self.hselk = dr("hselk", [128, 8])
        self.onesf = dr("onesf", [1, 128])
        self.onescf = dr("onescf", [128, 1])
        self.onescb = dr("onescb", [128, 1], BF16)
        self.pflagb = dr("pflagb", [128, 1], BF16)
        self.pflagf = dr("pflagf", [128, 1])
        self.pflag0f = dr("pflag0f", [128, 1])
        self.cbias = dr("cbias", [128, 8])
        self.out = dr("y", [D, T], F32, "ExternalOutput")
        self.blob_in = nc.dram_tensor("blob_in", [128, 8256], BF16).ap()
        self.blob_out = nc.dram_tensor("blob_out", [256, 8256], BF16).ap()
        self.blobf_in = nc.dram_tensor("blobf_in", [256, 2048], BF16).ap()
        self.blobf_out = nc.dram_tensor("blobf_out", [512, 2048], BF16).ap()

    def _mk_sbuf(self):
        A = lambda n, sh, dt: self.nc.alloc_sbuf_tensor(n, sh, dt)
        self.XU = [A(f"xu{c}", [128, T], F32) for c in range(NC)]
        self.SCR = A("scr", [128, 1024], F32)
        self.WST = A("wst", [128, 1024], F32)
        self.POOL = A("pool16", [16, 2048], F32)
        self.RW = A("rw", [128, 256], F32)
        self.CMB = A("cmb", [128, 24], F32)
        self.SIMN = A("simn", [128, 128], F32)
        self.GW1 = A("gwork", [128, 512], F32)
        self.PRB = [A(f"prb{i}", [128, 1536], BF16) for i in range(2)]
        self.AG = [A(f"ag{c}", [128, GW], BF16) for c in range(NC)]
        self.VB = A("vb", [128, self.VB_W], BF16)
        self.VF = A("vf", [128, self.VF_W], F32)
        self.c_identb = A("c_identb", [128, 128], BF16)
        self.c_identf = A("c_identf", [128, 128], F32)
        self.c_pswap = A("c_pswap", [128, 128], BF16)
        self.c_trail = A("c_trail", [128, 8 * GW], BF16)
        self.c_tril = A("c_tril", [128, 128], BF16)
        self.c_band = A("c_band", [128, 128], BF16)
        self.c_sel2 = A("c_sel2", [16, 8 * 128], F32)
        self.c_selk = A("c_selk", [4, 2 * 128], F32)
        self.c_sel4 = A("c_sel4", [4, 4 * 128], F32)
        self.c_tcsel = A("c_tcsel", [128, 16], F32)
        self.c_hsel = A("c_hsel", [128, 8 * 16], F32)
        self.c_hselk = A("c_hselk", [128, 8], F32)
        self.c_onesf = A("c_onesf", [1, 128], F32)
        self.c_onescf = A("c_onescf", [128, 1], F32)
        self.c_onescb = A("c_onescb", [128, 1], BF16)
        self.c_pflagb = A("c_pflagb", [128, 1], BF16)
        self.c_pflagf = A("c_pflagf", [128, 1], F32)
        self.c_pflag0f = A("c_pflag0f", [128, 1], F32)
        self.c_cbias = A("c_cbias", [128, 8], F32)
        self.c_n1w = A("c_n1w", [128, 8], F32)
        self.c_n2w = A("c_n2w", [128, 8], F32)
        self.TAB = A("tab", [128, 1024], F32)
        self.STG = A("stg", [1, 512], F32)

    def _mk_psum(self):
        nc = self.nc
        self.PSA = nc.alloc_psum_tensor("psa", [128, 1536], F32)
        self.PSB = nc.alloc_psum_tensor("psb", [128, 1536], F32)
        self.PSC = nc.alloc_psum_tensor("psc", [128, 512], F32)
        self.PSD = nc.alloc_psum_tensor("psd", [128, 512], F32)

    def MM(self, out, lhsT, rhs, start, stop):
        return lambda eng, o=out, l=lhsT, r=rhs, a=start, b=stop: \
            eng.matmul(o, l, r, start=a, stop=b, skip_group_check=True)

    def TR(self, out, in_, ident):
        return lambda eng, o=out, i=in_, d=ident: eng.transpose(o, i, d)

    def CP(self, out, in_):
        return lambda eng, o=out, i=in_: eng.tensor_copy(out=o, in_=i)

    def TT(self, out, a, b, op):
        return lambda eng, o=out, x=a, y=b, p=op: eng.tensor_tensor(out=o, in0=x, in1=y, op=p)

    def TS(self, out, in0, s1, op0):
        return lambda eng, o=out, i=in0, a=s1, p0=op0: \
            eng.tensor_single_scalar(out=o, in_=i, scalar=a, op=p0)

    def STT(self, out, in0, s, in1, op0, op1):
        return lambda eng, o=out, i=in0, a=s, j=in1, p0=op0, p1=op1: \
            eng.scalar_tensor_tensor(out=o, in0=i, scalar=a, in1=j, op0=p0, op1=p1)

    def ACTV(self, out, in_, fn, bias=0.0, scale=1.0, accum=None):
        return lambda eng, o=out, i=in_, f=fn, b=bias, sc=scale, ac=accum: \
            eng.activation(out=o, in_=i, func=f, bias=b, scale=sc, accum_out=ac)

    def RED(self, out, in_, op):
        return lambda eng, o=out, i=in_, p=op: eng.tensor_reduce(out=o, in_=i, axis=AXX, op=p)

    def RECIP(self, out, in_):
        return lambda eng, o=out, i=in_: eng.reciprocal(out=o, in_=i)

    def MS(self, ap, v):
        return lambda eng, a=ap, vv=v: eng.memset(a, vv)

    def DMA(self, out, in_):
        return lambda eng, o=out, i=in_: eng.dma_start(out=o, in_=i)

    def COLL(self, kind, ins, outs):
        return lambda eng, k=kind, i=ins, o=outs: eng.collective_compute(
            k, ALU.bypass, replica_groups=GROUPS, ins=i, outs=o)

    def stream_weight(self, src_slice, dst_bf, deps=()):
        P = self.P
        w = src_slice.shape[-1]
        assert w <= 512
        slot = self._wrr % 2
        epoch = (self._wrr // 1400) % 3
        self._wrr += 1
        stg = self.WST[:, slot * 512: slot * 512 + w]
        dd = list(deps)
        if slot in self._wslot_tok:
            dd.append(self._wslot_tok[slot])
        t_d = P.dma("y", self.DMA(stg, src_slice), deps=dd,
                    sem="yw%d" % (2 * epoch + slot))
        t_c = P.op("g", self.CP(dst_bf, stg), deps=[t_d])
        self._wslot_tok[slot] = t_c
        return t_c

    def rstd_chain(self, dst_sb, src_ps, scale, eps, tmp, deps=()):
        P = self.P
        t1 = P.op("s", self.ACTV(tmp, src_ps, AF.Ln, bias=float(eps), scale=float(scale)),
                  deps=deps)
        return P.op("s", self.ACTV(dst_sb, tmp, AF.Exp, scale=-0.5), deps=[t1])


PHASE_LIMIT = 99
LA_LVL = 9
SIMPLE_VCOPY = False
NODIV = False
GA_LVL = 9


def _finish_early(K):
    P = K.P
    P.barrier()
    for c in range(NC):
        for g in range(NG):
            P.dma("g", K.DMA(K.out[c * 128:(c + 1) * 128, g * GW:(g + 1) * GW],
                             K.XU[c][:, g * GW:(g + 1) * GW]), sem="go")
    P.barrier()
    K.P.items["g"].append(([("go", P.cnt["go"])], ("NOP",), None, 0))
    return K


def build_program():
    K = Builder()
    P = K.P
    VB, VF = K.VB, K.VF
    PSA, PSB, PSC, PSD = K.PSA, K.PSB, K.PSC, K.PSD

    for dst, src in [
        (K.c_identb[:], K.identb), (K.c_identf[:], K.identf),
        (K.c_pswap[:], K.pswap),
        (K.c_tril[:], K.trilm), (K.c_band[:], K.bandm),
        (K.c_sel2[:], K.sel2), (K.c_selk[:], K.selk), (K.c_sel4[:], K.sel4),
        (K.c_tcsel[:], K.tcsel),
        (K.c_hsel[:], K.hsel), (K.c_hselk[:], K.hselk),
        (K.c_onesf[:], K.onesf), (K.c_onescf[:], K.onescf),
        (K.c_onescb[:], K.onescb), (K.c_pflagb[:], K.pflagb),
        (K.c_pflagf[:], K.pflagf), (K.c_pflag0f[:], K.pflag0f),
        (K.c_cbias[:], K.cbias),
        (K.c_trail[:], K.trailm),
        (K.c_n1w[:], K.n1w), (K.c_n2w[:], K.n2w),
    ]:
        P.dma("y", K.DMA(dst, src))
    for c in range(NC):
        P.dma("y", K.DMA(K.XU[c][:], K.x[c * 128:(c + 1) * 128, :]))
    P.barrier()

    scr = lambda i: K.SCR[:, 512 * i:512 * (i + 1)]
    guard = {}

    def gdep(name):
        v = guard.get(name)
        if v is None:
            return []
        return list(v) if isinstance(v, list) else [v]

    def ln_pass_stats(get_x, with_mean):
        first = True
        last = []
        for tc in range(NG):
            for c in range(NC):
                xap = get_x(c, tc)
                lhs = K.c_tcsel[:, 4 * tc:4 * tc + 4]
                stop = (tc == NG - 1 and c == NC - 1)
                toks = []
                fdep = gdep("pscS") if first else []
                if with_mean:
                    toks.append(P.op("t", K.MM(PSC[0:4, 0:512], lhs, xap, first, stop),
                                     deps=fdep))
                sl = "sq%d" % (c % 2)
                tq = P.op("v", K.TT(scr(c % 2), xap, xap, ALU.mult), deps=gdep(sl))
                tm = P.op("t", K.MM(PSC[32:36, 0:512], lhs, scr(c % 2), first, stop),
                          deps=[tq] + fdep)
                guard[sl] = tm
                toks.append(tm)
                first = False
                last = toks
        return last

    if PHASE_LIMIT <= 0:
        return _finish_early(K)
    st = ln_pass_stats(lambda c, tc: K.XU[c][:, tc * GW:(tc + 1) * GW], True)
    S0 = K.POOL[0:4, 0:512]
    S1 = K.POOL[0:4, 512:1024]
    S2 = K.POOL[0:4, 1024:1536]
    S3 = K.POOL[0:4, 1536:2048]
    t_mean = P.op("v", K.TS(S0, PSC[0:4, 0:512], 1.0 / D, ALU.mult), deps=st)
    t_msq = P.op("v", K.TT(S2, S0, S0, ALU.mult), deps=[t_mean])
    t_var = P.op("v", K.STT(S1, PSC[32:36, 0:512], 1.0 / D, S2, ALU.mult, ALU.subtract),
                 deps=[t_msq] + st)
    t_rstd = K.rstd_chain(S3, S1, 1.0, EPS_LN, S2, deps=[t_var])
    t_k1 = P.op("v", K.TT(S2, S1, S3, ALU.mult), deps=[t_rstd, t_var])
    t_k2 = P.op("v", K.TT(S2, S2, S3, ALU.mult), deps=[t_k1])
    t_cl = P.op("s", K.ACTV(S1, S2, AF.Ln, bias=EPS_RMS), deps=[t_k2])
    t_crms = P.op("s", K.ACTV(S1, S1, AF.Exp, scale=-0.5), deps=[t_cl])
    guard["pscS"] = t_var
    if PHASE_LIMIT <= 0.3:
        return _finish_early(K)

    def proj_rope_qk(tc, xn_toks, wmat, kwmat, is_global):
        ctab = K.TAB[:, 0:512]
        stab = K.TAB[:, 512:1024]
        tg1 = P.dma("y", K.DMA(ctab, (K.ctab_g if is_global else K.ctab_l)[:, tc * GW:(tc + 1) * GW]),
                    deps=gdep("tab"), sem="yt")
        tg2 = P.dma("y", K.DMA(stab, (K.stab_g if is_global else K.stab_l)[:, tc * GW:(tc + 1) * GW]),
                    deps=gdep("tab"), sem="yt")
        tg1 = tg2
        kraw = {}
        for kc in range(2):
            wts, rhs_src, rhs_dep = [], [], []
            if is_global:
                for lc in range(2):
                    wdst = VB[:, K.WSM_O + 2048 + kc * 512 + lc * 128:
                             K.WSM_O + 2048 + kc * 512 + (lc + 1) * 128]
                    wts.append(K.stream_weight(
                        kwmat[lc * 128:(lc + 1) * 128, kc * 128:(kc + 1) * 128], wdst))
                    rhs_src.append((wdst, VB[:, K.LT_O + lc * 512:K.LT_O + (lc + 1) * 512]))
                    rhs_dep.append(xn_toks[("lt", lc)])
            else:
                for cp in range(NC):
                    wdst = VB[:, K.WSM_O + kc * 1024 + cp * 128:
                             K.WSM_O + kc * 1024 + (cp + 1) * 128]
                    wts.append(K.stream_weight(
                        kwmat[cp * 128:(cp + 1) * 128, kc * 128:(kc + 1) * 128], wdst))
                    rhs_src.append((wdst, VB[:, K.XN_O + cp * 512:K.XN_O + (cp + 1) * 512]))
                    rhs_dep.append(xn_toks[cp])
            ps = PSB[:, 512 * kc:512 * (kc + 1)]
            mm = None
            n = len(rhs_src)
            for i, (wdst, rhs) in enumerate(rhs_src):
                mm = P.op("t", K.MM(ps, wdst, rhs, i == 0, i == n - 1),
                          deps=[wts[i], rhs_dep[i]] + gdep("psb%d" % kc))
            guard["psb%d" % kc] = mm
            kst = VB[:, K.KST_O + kc * 512:K.KST_O + (kc + 1) * 512]
            ev = P.op("v", K.CP(kst, ps), deps=[mm])
            tq = P.op("v", K.TT(scr(0), kst, kst, ALU.mult), deps=[ev] + gdep("sq0"))
            smm = P.op("t", K.MM(PSC[0:4, 0:512], K.c_hselk[:, 4 * kc:4 * kc + 4],
                                 scr(0), kc == 0, kc == 1), deps=[tq] + gdep("pscS"))
            guard["sq0"] = smm
            kraw[kc] = (kst, ev, smm)
        krstd = VF[0:4, 3072:3584]
        kk = K.rstd_chain(krstd, PSC[0:4, 0:512], 1.0 / HD, EPS_RMS,
                          VF[0:4, 3584:4096], deps=[kraw[1][2]])
        guard["pscS"] = kk
        for kc in range(2):
            kst, ev, smm = kraw[kc]
            psw = PSB[:, 1024:1536]
            sw = P.op("t", K.MM(psw, K.c_pswap[:], kst, True, True),
                      deps=[ev, kk] + gdep("psb2"))
            bk = P.op("t", K.MM(PSD[:, 0:512], K.c_selk[:, 128 * kc:128 * (kc + 1)],
                                krstd, True, True), deps=[kk] + gdep("psd"))
            o1 = P.op("v", K.TT(scr(0), psw, stab, ALU.mult), deps=[sw, tg2] + gdep("sq0"))
            guard["psb2"] = o1
            o2 = P.op("v", K.TT(scr(1), kst, ctab, ALU.mult), deps=[tg1, ev] + gdep("sq1"))
            o3 = P.op("v", K.TT(scr(0), scr(0), scr(1), ALU.add), deps=[o1, o2])
            o4 = P.op("v", K.TT(kst, scr(0), PSD[:, 0:512], ALU.mult), deps=[o3, bk])
            guard["psd"] = o4
            guard["sq0"] = o4
            guard["sq1"] = o4
            P.dma("g", K.DMA(K.blob_in[:, 2048 * kc + 512 * tc:2048 * kc + 512 * (tc + 1)],
                             kst), deps=[o4])
        if PHASE_LIMIT <= 0.6:
            return
        qf = {}
        for f in range(NC):
            wts = []
            wdst = VB[:, K.WQF_O + (f % 2) * 1024: K.WQF_O + (f % 2) * 1024 + 1024]
            for cp in range(NC):
                wts.append(K.stream_weight(
                    wmat[cp * 128:(cp + 1) * 128, f * 128:(f + 1) * 128],
                    wdst[:, cp * 128:(cp + 1) * 128], deps=gdep("wqf%d" % (f % 2))))
            ps = PSA[:, 512 * (f % 3):512 * (f % 3) + 512]
            mm = None
            for cp in range(NC):
                mm = P.op("t", K.MM(ps, wdst[:, cp * 128:(cp + 1) * 128],
                                    VB[:, K.XN_O + cp * 512:K.XN_O + (cp + 1) * 512],
                                    cp == 0, cp == NC - 1),
                          deps=[wts[cp], xn_toks.get(cp)] + gdep("psa%d" % (f % 3)))
            guard["wqf%d" % (f % 2)] = mm
            qt = VB[:, K.QT_O + tc * 4096 + f * 512: K.QT_O + tc * 4096 + (f + 1) * 512]
            ev = P.op("v", K.CP(qt, ps), deps=[mm])
            guard["psa%d" % (f % 3)] = ev
            tq = P.op("v", K.TT(scr(0), qt, qt, ALU.mult), deps=[ev] + gdep("sq0"))
            smm = P.op("t", K.MM(PSC[0:16, 0:512], K.c_hsel[:, 16 * f:16 * (f + 1)],
                                 scr(0), f == 0, f == NC - 1),
                      deps=[tq] + gdep("pscS"))
            guard["sq0"] = smm
            qf[f] = (qt, ev, smm)
        qrstd = VF[0:16, 3072:3584]
        qq = K.rstd_chain(qrstd, PSC[0:16, 0:512], 1.0 / HD, EPS_RMS,
                          VF[0:16, 3584:4096], deps=[qf[NC - 1][2]])
        guard["pscS"] = qq
        for f in range(NC):
            qt, ev, smm = qf[f]
            ps = PSA[:, 512 * (f % 3):512 * (f % 3) + 512]
            sw = P.op("t", K.MM(ps, K.c_pswap[:], qt, True, True),
                      deps=[ev, qq] + gdep("psa%d" % (f % 3)))
            bq = P.op("t", K.MM(PSD[:, 0:512], K.c_sel2[:, 128 * f:128 * (f + 1)],
                                qrstd, True, True), deps=[qq] + gdep("psd"))
            o1 = P.op("v", K.TT(scr(0), ps, stab, ALU.mult), deps=[sw, tg2] + gdep("sq0"))
            guard["psa%d" % (f % 3)] = o1
            o2 = P.op("v", K.TT(scr(1), qt, ctab, ALU.mult), deps=[tg1, ev] + gdep("sq1"))
            o3 = P.op("v", K.TT(scr(0), scr(0), scr(1), ALU.add), deps=[o1, o2])
            o4 = P.op("v", K.TT(qt, scr(0), PSD[:, 0:512], ALU.mult), deps=[o3, bq])
            guard["psd"] = o4
            guard["sq0"] = o4
            guard["sq1"] = o4
        guard["tab"] = o4

    for tc in range(NG if PHASE_LIMIT > 0.7 else 1):
        s4 = K.c_sel4[:, 128 * tc:128 * (tc + 1)]
        bm = P.op("t", K.MM(PSA[:, 0:512], s4, S0, True, True),
                  deps=[t_mean] + gdep("psa0"))
        br = P.op("t", K.MM(PSA[:, 512:1024], s4, S3, True, True),
                  deps=[t_rstd] + gdep("psa1"))
        bc = P.op("t", K.MM(PSA[:, 1024:1536], s4, S1, True, True),
                  deps=[t_crms] + gdep("psa2"))
        xn_toks = {}
        for c in range(NC):
            xap = K.XU[c][:, tc * GW:(tc + 1) * GW]
            t1 = P.op("v", K.TT(scr(0), xap, PSA[:, 0:512], ALU.subtract),
                      deps=[bm] + gdep("sq0"))
            t2 = P.op("v", K.TT(xap, scr(0), PSA[:, 512:1024], ALU.mult),
                      deps=[t1, br])
            guard["sq0"] = t2
            xn = VB[:, K.XN_O + c * 512:K.XN_O + (c + 1) * 512]
            t3 = P.op("v", K.STT(xn, xap, K.c_n1w[:, c:c + 1], PSA[:, 1024:1536],
                                 ALU.mult, ALU.mult), deps=[t2, bc] + gdep("xn%d" % c))
            xn_toks[c] = t3
        guard["psa0"] = xn_toks[NC - 1]
        guard["psa1"] = xn_toks[NC - 1]
        guard["psa2"] = xn_toks[NC - 1]
        if PHASE_LIMIT <= 0.4:
            continue
        lt_toks = {}
        for lc in range(2):
            wts = []
            wdst = VB[:, K.WSM_O + lc * 1024:K.WSM_O + lc * 1024 + 1024]
            for cp in range(NC):
                wts.append(K.stream_weight(
                    K.kvd[cp * 128:(cp + 1) * 128, lc * 128:(lc + 1) * 128],
                    wdst[:, cp * 128:(cp + 1) * 128]))
            ps = PSB[:, 512 * lc:512 * (lc + 1)]
            mm = None
            for cp in range(NC):
                mm = P.op("t", K.MM(ps, wdst[:, cp * 128:(cp + 1) * 128],
                                    VB[:, K.XN_O + cp * 512:K.XN_O + (cp + 1) * 512],
                                    cp == 0, cp == NC - 1),
                          deps=[wts[cp], xn_toks[cp]] + gdep("psb%d" % lc))
            lt = VB[:, K.LT_O + lc * 512:K.LT_O + (lc + 1) * 512]
            tl = P.op("v", K.CP(lt, ps), deps=[mm] + gdep("lt%d" % lc))
            guard["psb%d" % lc] = tl
            lt_toks[("lt", lc)] = tl
            xn_toks[("lt", lc)] = tl
        if PHASE_LIMIT <= 0.5:
            continue
        vup_sl = []
        for lc in range(2):
            dst = VB[:, K.WSM_O + 3072 + lc * 256:K.WSM_O + 3072 + (lc + 1) * 256]
            tw = K.stream_weight(K.vup[lc * 128:(lc + 1) * 128, :], dst)
            vup_sl.append((dst, tw))
        vst = VB[:, K.VST_O:K.VST_O + 1040]
        if PHASE_LIMIT <= 0.51:
            continue
        msv = P.op("v", K.MS(vst.rearrange("p (t g w) -> p t g w", t=4, g=4)[:, :, :, 64:65], 1.0),
                   deps=[("gb", P.cnt["gb"])])
        if PHASE_LIMIT <= 0.52:
            continue
        ev_last = None
        for j in range(4):
            ps = PSC[:, 0:256] if j % 2 == 0 else PSD[:, 0:256]
            mm = None
            for lc in range(2):
                lt = VB[:, K.LT_O + lc * 512:K.LT_O + (lc + 1) * 512]
                mm = P.op("t", K.MM(ps, lt[:, j * 128:(j + 1) * 128], vup_sl[lc][0],
                                    lc == 0, lc == 1),
                          deps=[vup_sl[lc][1], lt_toks[("lt", lc)]] + gdep("psc_v%d" % (j % 2))
                          + (gdep("psd") if j % 2 == 1 else []))
            if PHASE_LIMIT <= 0.53:
                continue
            dstv = vst[:, 260 * j:260 * (j + 1)].rearrange("p (g w) -> p g w", g=4)[:, :, 0:64]
            srcv = ps.rearrange("p (g w) -> p g w", g=4)
            ev = P.op("v", K.CP(dstv, srcv), deps=[mm, msv])
            guard["psc_v%d" % (j % 2)] = ev
            if j % 2 == 1:
                guard["psd"] = ev
            if PHASE_LIMIT <= 0.54:
                continue
            ev_last = P.dma("g", K.DMA(
                K.blob_in[:, 4096 + 1040 * tc + 260 * j:4096 + 1040 * tc + 260 * (j + 1)],
                vst[:, 260 * j:260 * (j + 1)]), deps=[ev])
        if ev_last is not None:
            guard["vst"] = ev_last
        if PHASE_LIMIT <= 0.55:
            continue
        proj_rope_qk(tc, xn_toks, K.wq_g, K.kup, True)
        for c in range(NC):
            guard["xn%d" % c] = ("t", P.cnt["t"])
        guard["lt0"] = ("t", P.cnt["t"])
        guard["lt1"] = ("t", P.cnt["t"])

    def exchange_big():
        P.barrier()
        tcc = P.cc(K.COLL("AllGather", [K.blob_in.opt()], [K.blob_out.opt()]))
        for r in range(2):
            for sl in range(4):
                Sg = 2 * sl + r
                for kc in range(2):
                    P.dma("y", K.DMA(
                        VB[:, K.KVR_O + 4096 * kc + 512 * Sg:
                           K.KVR_O + 4096 * kc + 512 * (Sg + 1)],
                        K.blob_out[128 * r:128 * (r + 1),
                                   2048 * kc + 512 * sl:2048 * kc + 512 * (sl + 1)]),
                        deps=[tcc], sem="ye")
                P.dma("y", K.DMA(
                    VB[:, K.VG_O + 1040 * Sg:K.VG_O + 1040 * (Sg + 1)],
                    K.blob_out[128 * r:128 * (r + 1),
                               4096 + 1040 * sl:4096 + 1040 * (sl + 1)]),
                    deps=[tcc], sem="ye")
        P.barrier()

    exchange_big()
    if PHASE_LIMIT <= 2:
        return _finish_early(K)

    def kv_loc(s):
        g = SLOT_KV[s]
        return g // 2, 64 * (g % 2)

    def global_attention():
        last_exp = {}
        last_pv_rd = {}
        evac_tok = {}
        last_trc = None
        ng_lim = NG if PHASE_LIMIT > 2.8 else 1
        ns_lim = 16 if PHASE_LIMIT > 2.5 else (4 if PHASE_LIMIT > 2.16 else (2 if PHASE_LIMIT > 2.12 else 1))
        for g in range(ng_lim):
            J = 8 * g + 8
            if GA_LVL >= 1:
                vl = P.op("v", K.TS(VB[:, K.VLAST_O:K.VLAST_O + 1040],
                                    VB[:, K.VG_O + 1040 * (2 * g + 1):K.VG_O + 1040 * (2 * g + 2)],
                                    K.c_pflagf[:], ALU.mult),
                          deps=gdep("vlast"))
            else:
                vl = None
            strips = []
            i = 0
            while i < J:
                strips.append(list(range(i, min(i + 3, J))))
                i += 3
            for s in range(ns_lim):
                tidx, base = kv_loc(s)
                f = s // 2
                q_full = VB[:, K.QT_O + g * 4096 + f * 512:K.QT_O + g * 4096 + (f + 1) * 512]
                pv_ps = PSC[0:65, 0:512] if s % 2 == 0 else PSD[0:65, 0:512]
                pv_first = True
                for bi, chunks in enumerate(strips):
                    sp = bi % 2
                    strip = PSA if sp == 0 else PSB
                    prb = K.PRB[sp][:]
                    sc = []
                    dd = ([last_exp[sp]] if sp in last_exp else [])
                    for ci, ch in enumerate(chunks):
                        lhs = VB[:, K.KVR_O + 4096 * tidx + 128 * ch:
                                 K.KVR_O + 4096 * tidx + 128 * (ch + 1)][base:base + 64, :]
                        sc.append(P.op("t", K.MM(strip[:, 512 * ci:512 * (ci + 1)],
                                                 lhs, q_full[base:base + 64, :],
                                                 True, True), deps=dd))
                    if GA_LVL < 2:
                        last_exp[sp] = sc[-1]
                        continue
                    w = 512 * len(chunks)
                    te = P.op("s", K.ACTV(prb[:, 0:w], strip[:, 0:w], AF.Exp,
                                          bias=-8.0, scale=0.125),
                              deps=sc + ([last_pv_rd[sp]] if sp in last_pv_rd else []))
                    last_exp[sp] = te
                    if GA_LVL >= 3 and any(JJ >= J - 8 for JJ in chunks):
                        for ci2, ch2 in enumerate(chunks):
                            if ch2 >= J - 8:
                                jj = ch2 - (J - 8)
                                P.op("v", K.TT(prb[:, 512 * ci2:512 * (ci2 + 1)],
                                               prb[:, 512 * ci2:512 * (ci2 + 1)],
                                               K.c_trail[:, 512 * jj:512 * (jj + 1)],
                                               ALU.mult), deps=[te])
                        probs_rdy = ("v", P.cnt["v"])
                    else:
                        probs_rdy = te
                    if GA_LVL < 4:
                        continue
                    for ci, ch in enumerate(chunks):
                        last4 = ch >= J - 4
                        gk = SLOT_KV[s]
                        if last4:
                            vap = VB[:, K.VLAST_O + 260 * (ch % 4) + 65 * gk:
                                     K.VLAST_O + 260 * (ch % 4) + 65 * gk + 65]
                        else:
                            vap = VB[:, K.VG_O + 260 * ch + 65 * gk:
                                     K.VG_O + 260 * ch + 65 * gk + 65]
                        dd2 = [probs_rdy] + ([vl] if (last4 and vl) else [])
                        if pv_first and (s % 2) in evac_tok:
                            dd2.append(evac_tok[s % 2])
                        tpv = P.op("t", K.MM(pv_ps, vap, prb[:, 512 * ci:512 * (ci + 1)],
                                             pv_first, ch == J - 1), deps=dd2)
                        pv_first = False
                        last_pv_rd[sp] = tpv
                if GA_LVL < 5:
                    continue
                pv_t = tpv
                agc = K.AG[s // 2][:]
                tev = P.op("v", K.CP(agc[base:base + 64, :], pv_ps[0:64, :]), deps=[pv_t])
                tsr = P.op("v", K.CP(K.STG[:], pv_ps[64:65, 0:512]),
                           deps=[pv_t] + gdep("stg"))
                tdm = P.dma("g", K.DMA(K.POOL[s:s + 1, 0:512], K.STG[:]),
                            deps=[tsr] + ([last_trc] if last_trc else []), sem="gs")
                guard["stg"] = tdm
                evac_tok[s % 2] = tsr
            if GA_LVL < 6:
                continue
            sums_tok = ("gs", P.cnt["gs"])
            rec = K.POOL[0:16, 512:1024]
            trc = P.op("v", K.RECIP(rec, K.POOL[0:16, 0:512]), deps=[sums_tok])
            last_trc = trc
            norm_toks = []
            for c in range(NC):
                nps = PSC[:, 0:512] if c % 2 == 0 else PSD[:, 0:512]
                bb = P.op("t", K.MM(nps, K.c_sel2[:, 128 * c:128 * (c + 1)],
                                    rec, True, True),
                          deps=[trc, evac_tok.get(0), evac_tok.get(1)] +
                               gdep("nb%d" % (c % 2)))
                tn = P.op("v", K.TT(K.AG[c][:], K.AG[c][:], nps, ALU.mult),
                          deps=[bb])
                guard["nb%d" % (c % 2)] = tn
                norm_toks.append(tn)
            evac_tok[0] = norm_toks[-1]
            evac_tok[1] = norm_toks[-1]
            for co in range(NC if GA_LVL >= 7 else 0):
                wts = []
                wdst = VB[:, K.WOF_O + (co % 2) * 1024:K.WOF_O + (co % 2) * 1024 + 1024]
                for m in range(NC):
                    wts.append(K.stream_weight(
                        K.wo_g[m * 128:(m + 1) * 128, co * 128:(co + 1) * 128],
                        wdst[:, m * 128:(m + 1) * 128], deps=gdep("wof%d" % (co % 2))))
                ps = PSB[:, 0:512] if co % 2 == 0 else PSB[:, 512:1024]
                mm = None
                for m in range(NC):
                    mm = P.op("t", K.MM(ps, wdst[:, m * 128:(m + 1) * 128], K.AG[m][:],
                                        m == 0, m == NC - 1),
                              deps=[wts[m], norm_toks[m]] + gdep("wops%d" % (co % 2)))
                guard["wof%d" % (co % 2)] = mm
                tu = P.op("v", K.STT(K.XU[co][:, g * GW:(g + 1) * GW], ps, 1.0,
                                     K.XU[co][:, g * GW:(g + 1) * GW], ALU.mult, ALU.add),
                          deps=[mm])
                guard["wops%d" % (co % 2)] = tu
            guard["vlast"] = ("t", P.cnt["t"])
            last_exp[0] = ("v", P.cnt["v"])
            last_exp[1] = ("v", P.cnt["v"])
        P.barrier()

    global_attention()
    if PHASE_LIMIT <= 3:
        return _finish_early(K)

    ln_pass_stats(lambda c, tc: K.XU[c][:, tc * GW:(tc + 1) * GW], False)
    r2 = K.POOL[0:4, 1536:2048]
    t_r2 = K.rstd_chain(r2, PSC[32:36, 0:512], 1.0 / D, EPS_RMS,
                        K.POOL[0:4, 1024:1536], deps=[("t", P.cnt["t"])])
    guard["pscS"] = t_r2
    if PHASE_LIMIT <= 3.1:
        return _finish_early(K)
    for tc in range(NG if PHASE_LIMIT > 3.5 else 1):
        s4 = K.c_sel4[:, 128 * tc:128 * (tc + 1)]
        br = P.op("t", K.MM(PSA[:, 1024:1536], s4, r2, True, True),
                  deps=[t_r2] + gdep("psa2"))
        xn_toks = {}
        for c in range(NC):
            xap = K.XU[c][:, tc * GW:(tc + 1) * GW]
            xn = VB[:, K.XN_O + c * 512:K.XN_O + (c + 1) * 512]
            t3 = P.op("v", K.STT(xn, xap, K.c_n2w[:, c:c + 1], PSA[:, 1024:1536],
                                 ALU.mult, ALU.mult), deps=[br] + gdep("xn%d" % c))
            xn_toks[c] = t3
        guard["psa2"] = xn_toks[NC - 1]
        wv_sl = []
        for cp in range(NC):
            dst = VB[:, K.WSM_O + 2048 + cp * 256:K.WSM_O + 2048 + (cp + 1) * 256]
            tw = K.stream_weight(K.wv_l[cp * 128:(cp + 1) * 128, :], dst)
            wv_sl.append((dst, tw))
        vst = VB[:, K.VST_O:K.VST_O + 1040]
        msv = P.op("v", K.MS(vst.rearrange("p (t g w) -> p t g w", t=4, g=4)[:, :, :, 64:65], 1.0),
                   deps=[("gb", P.cnt["gb"])])
        ev_last = None
        for j in range(4):
            ps = PSC[:, 0:256] if j % 2 == 0 else PSD[:, 0:256]
            mm = None
            for cp in range(NC):
                mm = P.op("t", K.MM(
                    ps, VB[:, K.XN_O + cp * 512 + j * 128:K.XN_O + cp * 512 + (j + 1) * 128],
                    wv_sl[cp][0], cp == 0, cp == NC - 1),
                    deps=[wv_sl[cp][1], xn_toks[cp]] + gdep("psc_v%d" % (j % 2))
                    + (gdep("psd") if j % 2 == 1 else []))
            dstv = vst[:, 260 * j:260 * (j + 1)].rearrange("p (g w) -> p g w", g=4)[:, :, 0:64]
            ev = P.op("v", K.CP(dstv, ps.rearrange("p (g w) -> p g w", g=4)),
                      deps=[mm, msv])
            guard["psc_v%d" % (j % 2)] = ev
            if j % 2 == 1:
                guard["psd"] = ev
            ev_last = P.dma("g", K.DMA(
                K.blob_in[:, 4096 + 1040 * tc + 260 * j:4096 + 1040 * tc + 260 * (j + 1)],
                vst[:, 260 * j:260 * (j + 1)]), deps=[ev])
        guard["vst"] = ev_last
        proj_rope_qk(tc, xn_toks, K.wq_l, K.wk_l, False)
        for c in range(NC):
            guard["xn%d" % c] = ("t", P.cnt["t"])

    if PHASE_LIMIT <= 4:
        return _finish_early(K)
    exchange_big()
    if PHASE_LIMIT <= 5:
        return _finish_early(K)


    def local_attention():
        last_exp = {}
        last_pv_rd = {}
        evac_tok = {}
        last_trc = None
        xd_tok = {}
        f1c = K.c_pflagf[:]
        f0c = K.c_pflag0f[:]
        kb = {}
        vb_tok = None
        g0_rd = []
        for g in range(NG):
            if g >= 1 and LA_LVL >= 1:
                QB = K.QT_O + 4096 * (g - 1)
                tdep = [("t", P.cnt["t"])]
                for tidx in range(2):
                    src = K.KVR_O + 4096 * tidx
                    dst = VB[:, QB + 768 * tidx:QB + 768 * tidx + 768]
                    t1 = P.op("v", K.TS(dst,
                                        VB[:, src + 128 * (8 * g + 2):src + 128 * (8 * g + 8)],
                                        f1c, ALU.mult), deps=tdep)
                    kb[tidx] = P.op("v", K.STT(dst,
                                               VB[:, src + 128 * (8 * g - 2):src + 128 * (8 * g + 4)],
                                               f0c, dst, ALU.mult, ALU.add), deps=[t1])
                dstv = VB[:, QB + 1536:QB + 1536 + 1560]
                t1 = P.op("v", K.TS(dstv,
                                    VB[:, K.VG_O + 260 * (8 * g + 2):K.VG_O + 260 * (8 * g + 8)],
                                    f1c, ALU.mult), deps=tdep)
                vb_tok = P.op("v", K.STT(dstv,
                                         VB[:, K.VG_O + 260 * (8 * g - 2):K.VG_O + 260 * (8 * g + 4)],
                                         f0c, dstv, ALU.mult, ALU.add), deps=[t1])
            for j in range(4):
                if LA_LVL < 1:
                    continue
                nch = 3
                for hb in range(4):
                    grp = [[0, 2, 4, 6], [1, 3, 5, 7],
                           [8, 10, 12, 14], [9, 11, 13, 15]][hb]
                    half = hb // 2
                    if g == 0 and hb % 2 == 0:
                        tdep = list(g0_rd)
                        src = K.KVR_O + 4096 * half
                        u0 = max(0, 2 - j)
                        dstk = VB[:, K.VLAST_O:K.VLAST_O + 384]
                        t1 = P.op("v", K.TS(dstk,
                                            VB[:, src + 128 * (j + 2):src + 128 * (j + 5)],
                                            f1c, ALU.mult), deps=tdep)
                        kb = {half: P.op("v", K.STT(
                            VB[:, K.VLAST_O + 128 * u0:K.VLAST_O + 384],
                            VB[:, src + 128 * (j + u0 - 2):src + 128 * (j + 1)],
                            f0c, VB[:, K.VLAST_O + 128 * u0:K.VLAST_O + 384],
                            ALU.mult, ALU.add), deps=[t1])}
                        dv = VB[:, K.VLAST_O + 384:K.VLAST_O + 774].rearrange(
                            "p (u w) -> p u w", u=3)
                        sv1 = VB[:, K.VG_O + 260 * (j + 2):K.VG_O + 260 * (j + 5)].rearrange(
                            "p (u w) -> p u w", u=3)[:, :, 130 * half:130 * half + 130]
                        t1 = P.op("v", K.TS(dv, sv1, f1c, ALU.mult), deps=tdep)
                        dv0 = VB[:, K.VLAST_O + 384 + 130 * u0:K.VLAST_O + 774].rearrange(
                            "p (u w) -> p u w", u=3 - u0)
                        sv0 = VB[:, K.VG_O + 260 * (j + u0 - 2):K.VG_O + 260 * (j + 1)].rearrange(
                            "p (u w) -> p u w", u=3 - u0)[:, :, 130 * half:130 * half + 130]
                        vb_tok = P.op("v", K.STT(dv0, sv0, f0c, dv0, ALU.mult, ALU.add),
                                      deps=[t1])
                    sp = hb % 2
                    strip = PSA if sp == 0 else PSB
                    prb = K.PRB[sp][:]
                    sc = []
                    dd = ([last_exp[sp]] if sp in last_exp else [])
                    for si, s in enumerate(grp):
                        tidx, base = kv_loc(s)
                        f = s // 2
                        qap = VB[:, K.QT_O + g * 4096 + f * 512 + j * 128:
                                 K.QT_O + g * 4096 + f * 512 + (j + 1) * 128][base:base + 64, :]
                        for ci in range(nch):
                            if g == 0:
                                lhs = VB[:, K.VLAST_O + 128 * ci:
                                         K.VLAST_O + 128 * (ci + 1)][base:base + 64, :]
                            else:
                                ko = K.QT_O + 4096 * (g - 1) + 768 * tidx + 128 * (j + ci)
                                lhs = VB[:, ko:ko + 128][base:base + 64, :]
                            sc.append(P.op("t", K.MM(
                                strip[:, 384 * si + 128 * ci:384 * si + 128 * (ci + 1)],
                                lhs, qap, True, True), deps=dd + [kb[tidx]]))
                    if LA_LVL < 2:
                        last_exp[sp] = sc[-1]
                        continue
                    te = P.op("s", K.ACTV(prb[:, 0:1536], strip[:, 0:1536], AF.Exp,
                                          bias=-8.0, scale=0.125),
                              deps=sc + ([last_pv_rd[sp]] if sp in last_pv_rd else []))
                    last_exp[sp] = te
                    if LA_LVL < 3:
                        continue
                    mt = [te, vb_tok]
                    for si in range(4):
                        for ci in range(nch):
                            m = None
                            if ci == 2:
                                m = K.c_tril[:]
                            elif ci == 0:
                                m = K.c_band[:]
                            if m is not None:
                                mt.append(P.op("v", K.TT(
                                    prb[:, 384 * si + 128 * ci:384 * si + 128 * (ci + 1)],
                                    prb[:, 384 * si + 128 * ci:384 * si + 128 * (ci + 1)],
                                    m, ALU.mult), deps=[te]))
                    if LA_LVL < 4:
                        continue
                    for si, s in enumerate(grp):
                        gk = SLOT_KV[s]
                        base = 64 * (s % 2)
                        pv_ps = PSC[0:65, 0:128] if si % 2 == 0 else PSD[0:65, 0:128]
                        mm = None
                        for ci in range(nch):
                            if g == 0:
                                vo = K.VLAST_O + 384 + 130 * ci + 65 * (gk - 2 * half)
                            else:
                                vo = (K.QT_O + 4096 * (g - 1) + 1536
                                      + 260 * (j + ci) + 65 * gk)
                            vap = VB[:, vo:vo + 65]
                            dd2 = list(mt)
                            if ci == 0 and (si % 2) in evac_tok:
                                dd2.append(evac_tok[si % 2])
                            mm = P.op("t", K.MM(
                                pv_ps, vap,
                                prb[:, 384 * si + 128 * ci:384 * si + 128 * (ci + 1)],
                                ci == 0, ci == nch - 1), deps=dd2)
                            last_pv_rd[sp] = mm
                        if LA_LVL < 5:
                            evac_tok[si % 2] = mm
                            continue
                        agc = K.AG[s // 2][:]
                        tev = P.op("v", K.CP(agc[base:base + 64, 128 * j:128 * (j + 1)],
                                             pv_ps[0:64, :]), deps=[mm])
                        tsr = P.op("v", K.CP(K.STG[:, 0:128], pv_ps[64:65, 0:128]),
                                   deps=[mm] + gdep("stgl"))
                        tdm = P.dma("g", K.DMA(K.POOL[s:s + 1, 128 * j:128 * (j + 1)],
                                               K.STG[:, 0:128]),
                                    deps=[tsr] + ([last_trc] if last_trc else []), sem="gs")
                        guard["stgl"] = tdm
                        evac_tok[si % 2] = tsr
                    if g == 0 and hb % 2 == 1:
                        g0_rd = [("t", P.cnt["t"])]
            if GA_LVL < 6 or LA_LVL < 6:
                continue
            sums_tok = ("gs", P.cnt["gs"])
            rec = K.POOL[0:16, 512:1024]
            trc = P.op("v", K.RECIP(rec, K.POOL[0:16, 0:512]), deps=[sums_tok])
            last_trc = trc
            norm_toks = []
            for c in range(NC):
                nps = PSC[:, 0:512] if c % 2 == 0 else PSD[:, 0:512]
                bb = P.op("t", K.MM(nps, K.c_sel2[:, 128 * c:128 * (c + 1)],
                                    rec, True, True),
                          deps=[trc, evac_tok.get(0), evac_tok.get(1)] +
                               gdep("nb%d" % (c % 2)))
                tn = P.op("v", K.TT(K.AG[c][:], K.AG[c][:], nps, ALU.mult),
                          deps=[bb])
                guard["nb%d" % (c % 2)] = tn
                norm_toks.append(tn)
            evac_tok[0] = norm_toks[-1]
            evac_tok[1] = norm_toks[-1]
            if LA_LVL < 7:
                continue
            for co in range(NC):
                wts = []
                wdst = VB[:, K.WOF_O + (co % 2) * 1024:K.WOF_O + (co % 2) * 1024 + 1024]
                for m in range(NC):
                    wts.append(K.stream_weight(
                        K.wo_l[m * 128:(m + 1) * 128, co * 128:(co + 1) * 128],
                        wdst[:, m * 128:(m + 1) * 128], deps=gdep("wof%d" % (co % 2))))
                ps = PSB[:, 0:512] if co % 2 == 0 else PSB[:, 512:1024]
                mm = None
                for m in range(NC):
                    mm = P.op("t", K.MM(ps, wdst[:, m * 128:(m + 1) * 128], K.AG[m][:],
                                        m == 0, m == NC - 1),
                              deps=[wts[m], norm_toks[m]] + gdep("wops%d" % (co % 2)))
                guard["wof%d" % (co % 2)] = mm
                xd = VF[:, Builder.XD_O + (co % 2) * 512:Builder.XD_O + (co % 2) * 512 + 512]
                td = P.dma("y", K.DMA(xd, K.x[co * 128:(co + 1) * 128, g * GW:(g + 1) * GW]),
                           deps=([xd_tok[co % 2]] if (co % 2) in xd_tok else []),
                           sem="yd%d" % (co % 2))
                t1 = P.op("v", K.STT(scr(0), ps, 1.0, K.XU[co][:, g * GW:(g + 1) * GW],
                                     ALU.mult, ALU.add), deps=[mm] + gdep("sq0"))
                t2 = P.op("v", K.TT(K.XU[co][:, g * GW:(g + 1) * GW], scr(0), xd, ALU.add),
                          deps=[t1, td])
                guard["sq0"] = t2
                guard["wops%d" % (co % 2)] = t2
                xd_tok[co % 2] = t2
            last_exp[0] = ("v", P.cnt["v"])
            last_exp[1] = ("v", P.cnt["v"])

    local_attention()
    P.barrier()
    if PHASE_LIMIT <= 6:
        return _finish_early(K)

    sim_toks = []
    for c in range(NC):
        sim_toks.append(P.dma("y", K.DMA(K.GW1[:, 16 * c:16 * (c + 1)],
                                         K.sim[c * 128:(c + 1) * 128, :])))
    simsq = None
    for c in range(NC):
        tq = P.op("v", K.TT(K.GW1[:, 128 + 16 * c:128 + 16 * (c + 1)],
                            K.GW1[:, 16 * c:16 * (c + 1)],
                            K.GW1[:, 16 * c:16 * (c + 1)], ALU.mult), deps=sim_toks)
        simsq = P.op("t", K.MM(PSC[0:1, 0:16], K.c_onescf[:],
                               K.GW1[:, 128 + 16 * c:128 + 16 * (c + 1)],
                               c == 0, c == NC - 1), deps=[tq])
    srow = K.POOL[0:1, 0:16]
    t_sr = K.rstd_chain(srow, PSC[0:1, 0:16], 1.0, 1e-12, K.POOL[0:1, 16 + 16:48],
                        deps=[simsq])
    bcs = P.op("t", K.MM(PSD[:, 0:16], K.c_onesf[:], srow, True, True), deps=[t_sr])
    simn_toks = []
    for c in range(NC):
        simn_toks.append(P.op("v", K.TT(K.SIMN[:, 16 * c:16 * (c + 1)],
                                        K.GW1[:, 16 * c:16 * (c + 1)], PSD[:, 0:16],
                                        ALU.mult), deps=[bcs]))
    tgx = P.dma("y", K.DMA(K.POOL[0:1, 64:80], K.gates), deps=[sim_toks[-1]])
    tg1 = P.op("s", K.ACTV(K.POOL[0:1, 80:96], K.POOL[0:1, 64:80], AF.Exp, scale=-1.0),
               deps=[tgx])
    tg2 = P.op("v", K.TS(K.POOL[0:1, 80:96], K.POOL[0:1, 80:96], 1.0, ALU.add), deps=[tg1])
    tg3 = P.op("v", K.RECIP(K.POOL[0:1, 96:112], K.POOL[0:1, 80:96]), deps=[tg2])
    bsg = P.op("t", K.MM(PSD[:, 16:32], K.c_onesf[:], K.POOL[0:1, 96:112], True, True),
               deps=[tg3])
    sgb = P.op("v", K.CP(K.GW1[:, 256:272], PSD[:, 16:32]), deps=[bsg])
    guard["psd"] = sgb
    guard["pscS"] = t_sr

    st2 = ln_pass_stats(lambda c, tc: K.XU[c][:, tc * GW:(tc + 1) * GW], True)
    t_mean = P.op("v", K.TS(S0, PSC[0:4, 0:512], 1.0 / D, ALU.mult),
                  deps=st2 + [bcs, bsg])
    t_msq = P.op("v", K.TT(S2, S0, S0, ALU.mult), deps=[t_mean])
    t_var = P.op("v", K.STT(S1, PSC[32:36, 0:512], 1.0 / D, S2, ALU.mult, ALU.subtract),
                 deps=[t_msq] + st2)
    guard["pscS"] = t_var
    t_rstd = K.rstd_chain(S3, S1, 1.0, EPS_LN, S2, deps=[t_var])
    t_cb1 = P.op("s", K.ACTV(S2, S1, AF.Ln, bias=0.0, scale=float(D)), deps=[t_var, t_rstd])
    t_cb2 = P.op("s", K.ACTV(S2, S2, AF.Exp, scale=-0.5), deps=[t_cb1])
    cmb_toks = {}
    for t_ in range(16):
        tcq, jq = t_ // 4, t_ % 4
        tr = P.op("t", K.TR(PSD[0:128, 0:4], S2[0:4, 128 * jq:128 * (jq + 1)],
                            K.c_identf[0:4, 0:4]), deps=[t_cb2] + gdep("psd"))
        cmb_toks[t_] = P.op("v", K.CP(K.CMB[:, t_:t_ + 1], PSD[0:128, tcq:tcq + 1]),
                            deps=[tr])
        guard["psd"] = cmb_toks[t_]
    w0 = 288
    rw_toks = {}
    prev_gate = None
    for tc in range(NG):
        s4 = K.c_sel4[:, 128 * tc:128 * (tc + 1)]
        bm = P.op("t", K.MM(PSA[:, 0:512], s4, S0, True, True),
                  deps=[t_mean] + gdep("psa0"))
        brr = P.op("t", K.MM(PSA[:, 512:1024], s4, S3, True, True),
                   deps=[t_rstd] + gdep("psa1"))
        mmL = None
        for c in range(NC):
            xap = K.XU[c][:, tc * GW:(tc + 1) * GW]
            t1 = P.op("v", K.TT(scr(0), xap, PSA[:, 0:512], ALU.subtract),
                      deps=[bm] + gdep("sq0"))
            xn = VB[:, K.XN_O + c * T + tc * 512:K.XN_O + c * T + (tc + 1) * 512]
            t2 = P.op("v", K.TT(xn, scr(0), PSA[:, 512:1024], ALU.mult), deps=[t1, brr])
            mmL = P.op("t", K.MM(PSC[0:16, 0:512], K.SIMN[:, 16 * c:16 * (c + 1)],
                                 scr(0), c == 0, c == NC - 1),
                       deps=[t1] + simn_toks + (gdep("pscS") if c == 0 else []))
            guard["sq0"] = [t2, mmL]
        lg_ap = VF[0:16, Builder.LGT_O:Builder.LGT_O + 512]
        tev = P.op("v", K.CP(lg_ap, PSC[0:16, 0:512]), deps=[mmL] + gdep("lgt"))
        guard["pscS"] = tev
        guard["psa0"] = tev
        guard["psa1"] = tev
        for jq in range(4):
            t_ = 4 * tc + jq
            tr = P.op("t", K.TR(PSD[0:128, 0:16],
                                VF[0:16, Builder.LGT_O + 128 * jq:
                                   Builder.LGT_O + 128 * (jq + 1)],
                                K.c_identf[0:16, 0:16]),
                      deps=[tev] + gdep("psd"))
            lt = K.GW1[:, 272:288]
            tcp0 = P.op("v", K.CP(lt, PSD[0:128, 0:16]),
                        deps=[tr] + ([prev_gate] if prev_gate else []))
            guard["psd"] = tcp0
            tsc = P.op("v", K.TS(lt, lt, K.CMB[:, t_:t_ + 1], ALU.mult),
                       deps=[tcp0, cmb_toks[t_]])
            g1 = K.GW1[:, w0:w0 + 16]
            t1 = P.op("v", K.TT(g1, lt, K.GW1[:, 256:272], ALU.subtract), deps=[tsc, sgb])
            gated = K.GW1[:, w0 + 16:w0 + 32]
            t2 = P.op("v", K.TS(gated, g1, 0.0, ALU.max), deps=[t1])
            mA = K.GW1[:, w0 + 32:w0 + 48]
            t3 = P.op("v", K.TS(mA, g1, 0.0, ALU.is_gt), deps=[t1])
            cnt_ = K.GW1[:, w0 + 48:w0 + 49]
            t4 = P.op("v", K.RED(cnt_, mA, ALU.add), deps=[t3])
            inact = K.GW1[:, w0 + 49:w0 + 50]
            t5 = P.op("v", K.TS(inact, cnt_, 0.0, ALU.is_equal), deps=[t4])
            m1 = K.GW1[:, w0 + 50:w0 + 51]
            t6 = P.op("v", K.RED(m1, g1, ALU.max), deps=[t1])
            oh1 = K.GW1[:, w0 + 64:w0 + 80]
            t7 = P.op("v", K.TS(oh1, g1, m1, ALU.is_equal), deps=[t6])
            lg2 = K.GW1[:, w0 + 80:w0 + 96]
            t8 = P.op("v", K.STT(lg2, oh1, -1e30, g1, ALU.mult, ALU.add), deps=[t7])
            m2 = K.GW1[:, w0 + 51:w0 + 52]
            t9 = P.op("v", K.RED(m2, lg2, ALU.max), deps=[t8])
            oh2 = K.GW1[:, w0 + 96:w0 + 112]
            t10 = P.op("v", K.TS(oh2, lg2, m2, ALU.is_equal), deps=[t9])
            fb = K.GW1[:, w0 + 112:w0 + 128]
            t11 = P.op("v", K.TT(fb, oh1, oh2, ALU.add), deps=[t10, t7])
            mask = K.GW1[:, w0 + 128:w0 + 144]
            t12 = P.op("v", K.STT(mask, fb, inact, mA, ALU.mult, ALU.add),
                       deps=[t11, t5, t3])
            ex = K.GW1[:, w0 + 144:w0 + 160]
            t13 = P.op("s", K.ACTV(ex, gated, AF.Exp), deps=[t2])
            em = K.GW1[:, w0 + 160:w0 + 176]
            t14 = P.op("v", K.TT(em, ex, mask, ALU.mult), deps=[t13, t12])
            ssum = K.GW1[:, w0 + 52:w0 + 53]
            t15 = P.op("v", K.RED(ssum, em, ALU.add), deps=[t14])
            rs = K.GW1[:, w0 + 53:w0 + 54]
            t16 = P.op("v", K.RECIP(rs, ssum), deps=[t15])
            t17 = P.op("v", K.TS(K.RW[:, 16 * t_:16 * (t_ + 1)], em, rs, ALU.mult),
                       deps=[t16, t14])
            rw_toks[t_] = t17
            guard["lgt"] = tr
            prev_gate = t17
    P.barrier()

    if PHASE_LIMIT <= 7:
        return _finish_early(K)
    for g in range(NG):
        acc_tok = {}
        for j in range(4):
            acc_tok[j] = P.op("v", K.MS(VF[:, Builder.ACC_O + 384 * j:
                                          Builder.ACC_O + 384 * (j + 1)], 0.0),
                              deps=gdep("acc%d" % j))
        ps_tok = {}
        for e in range(E):
            wts = []
            wdst = VB[:, K.RBW_O + (e % 2) * 3072:K.RBW_O + (e % 2) * 3072 + 3072]
            for cp in range(NC):
                wts.append(K.stream_weight(K.qkvf[e, cp * 128:(cp + 1) * 128, :],
                                           wdst[:, cp * 384:(cp + 1) * 384],
                                           deps=gdep("rbw%d" % (e % 2))))
            last_w = None
            for j in range(4):
                t_ = 4 * g + j
                ps = PSA[:, 0:384] if j % 2 == 0 else PSA[:, 512:896]
                mm = None
                for cp in range(NC):
                    lhs = VB[:, K.XN_O + cp * T + 512 * g + 128 * j:
                             K.XN_O + cp * T + 512 * g + 128 * (j + 1)]
                    mm = P.op("t", K.MM(ps, lhs, wdst[:, cp * 384:(cp + 1) * 384],
                                        cp == 0, cp == NC - 1),
                              deps=[wts[cp]] + ([ps_tok[j % 2]] if (j % 2) in ps_tok else []))
                st = P.op("v", K.STT(VF[:, Builder.ACC_O + 384 * j:Builder.ACC_O + 384 * (j + 1)],
                                     ps, K.RW[:, 16 * t_ + e:16 * t_ + e + 1],
                                     VF[:, Builder.ACC_O + 384 * j:Builder.ACC_O + 384 * (j + 1)],
                                     ALU.mult, ALU.add),
                          deps=[mm, acc_tok[j], rw_toks[t_]])
                acc_tok[j] = st
                ps_tok[j % 2] = st
                last_w = mm
            guard["rbw%d" % (e % 2)] = last_w
        for j in range(4):
            t_ = 4 * g + j
            acc = VF[:, Builder.ACC_O + 384 * j:Builder.ACC_O + 384 * (j + 1)]
            tv = P.op("v", K.CP(VB[:, K.VFT_O + 128 * t_:K.VFT_O + 128 * (t_ + 1)],
                                acc[:, 256:384]), deps=[acc_tok[j]])
            tprev = tv
            for off, dsto in ((0, K.QF_O), (128, K.KFT_O)):
                stg = VB[:, K.QTK_O + (0 if off == 0 else 128):
                         K.QTK_O + (128 if off == 0 else 256)]
                tcst = P.op("v", K.CP(stg, acc[:, off:off + 128]),
                            deps=[acc_tok[j], tprev] + gdep("qtk%d" % (off // 128)))
                ttr = P.op("t", K.TR(PSD[:, 0:64].bitcast(BF16), stg, K.c_identb[:]),
                           deps=[tcst] + gdep("psd"))
                tcp = P.op("v", K.CP(VB[:, dsto + 128 * t_:dsto + 128 * (t_ + 1)],
                                     PSD[:, 0:64].bitcast(BF16)), deps=[ttr])
                guard["psd"] = tcp
                guard["qtk%d" % (off // 128)] = ttr
                tprev = tcp
            acc_tok[j] = tprev
        for j in range(4):
            guard["acc%d" % j] = acc_tok[j]
    P.barrier()
    P.dma("g", K.DMA(K.blobf_in[0:128, :], VB[:, K.KFT_O:K.KFT_O + 2048]))
    P.dma("g", K.DMA(K.blobf_in[128:256, :], VB[:, K.VFT_O:K.VFT_O + 2048]))
    P.barrier()
    tcc = P.cc(K.COLL("AllGather", [K.blobf_in.opt()], [K.blobf_out.opt()]))
    for r in range(2):
        for sl in range(4):
            Sg = 2 * sl + r
            P.dma("y", K.DMA(VB[:, K.KF_O + 512 * Sg:K.KF_O + 512 * (Sg + 1)],
                             K.blobf_out[256 * r:256 * r + 128, 512 * sl:512 * (sl + 1)]),
                  deps=[tcc], sem="ye")
            P.dma("y", K.DMA(VB[:, K.VGF_O + 512 * Sg:K.VGF_O + 512 * (Sg + 1)],
                             K.blobf_out[256 * r + 128:256 * (r + 1),
                                         512 * sl:512 * (sl + 1)]),
                  deps=[tcc], sem="ye")
    P.barrier()

    if PHASE_LIMIT <= 8:
        return _finish_early(K)
    qn_toks = []
    for tc in range(NG):
        tq = P.op("v", K.TT(scr(0), VB[:, K.QF_O + 512 * tc:K.QF_O + 512 * (tc + 1)],
                            VB[:, K.QF_O + 512 * tc:K.QF_O + 512 * (tc + 1)], ALU.mult),
                  deps=gdep("sq0"))
        tm = P.op("t", K.MM(PSC[0:1, 0:512], K.c_onescf[:], scr(0), True, True), deps=[tq])
        guard["sq0"] = tm
        qn_toks.append(P.op("v", K.CP(K.POOL[0:1, 512 * tc:512 * (tc + 1)],
                                      PSC[0:1, 0:512]), deps=[tm]))
        guard["pscS"] = qn_toks[-1]
    tqm = P.op("v", K.RED(K.GW1[0:1, 0:1], K.POOL[0:1, 0:2048], ALU.max), deps=qn_toks)
    km = K.GW1[0:1, 1:2]
    prev = P.op("v", K.MS(km, 0.0))
    for kc in range(8):
        tq = P.op("v", K.TT(scr(0), VB[:, K.KF_O + 512 * kc:K.KF_O + 512 * (kc + 1)],
                            VB[:, K.KF_O + 512 * kc:K.KF_O + 512 * (kc + 1)], ALU.mult),
                  deps=gdep("sq0"))
        tm = P.op("t", K.MM(PSC[0:1, 0:512], K.c_onescf[:], scr(0), True, True), deps=[tq])
        guard["sq0"] = tm
        tcp = P.op("v", K.CP(K.STG[:], PSC[0:1, 0:512]), deps=[tm] + gdep("stg"))
        guard["pscS"] = tcp
        tmx = P.op("v", K.RED(K.GW1[0:1, 2:3], K.STG[:], ALU.max), deps=[tcp])
        guard["stg"] = tmx
        prev = P.op("v", K.TT(km, km, K.GW1[0:1, 2:3], ALU.max), deps=[tmx, prev])
    t_b1 = P.op("v", K.TT(K.GW1[0:1, 3:4], K.GW1[0:1, 0:1], km, ALU.mult),
                deps=[tqm, prev])
    t_b2 = P.op("s", K.ACTV(K.GW1[0:1, 4:5], K.GW1[0:1, 3:4], AF.Ln, bias=1e-20),
                deps=[t_b1])
    t_b3 = P.op("s", K.ACTV(K.GW1[0:1, 5:6], K.GW1[0:1, 4:5], AF.Exp, scale=0.5),
                deps=[t_b2])
    t_b4 = P.op("v", K.TS(K.GW1[0:1, 6:7], K.GW1[0:1, 5:6], -1.0 / float(np.sqrt(FHD)),
                          ALU.mult), deps=[t_b3])
    t_bb = P.op("t", K.MM(PSD[:, 0:1], K.c_onesf[:], K.GW1[0:1, 6:7], True, True),
                deps=[t_b4] + gdep("psd"))
    t_bc = P.op("v", K.CP(K.CMB[:, 16:17], PSD[:, 0:1]), deps=[t_bb])
    guard["psd"] = t_bc

    last_exp = {}
    last_pv_rd = {}
    aot_tok = None
    for g in range(NG):
        J = 8 * g + 8
        vl = P.op("v", K.TS(VB[:, K.VLASTF_O:K.VLASTF_O + 512],
                            VB[:, K.VGF_O + 512 * (2 * g + 1):K.VGF_O + 512 * (2 * g + 2)],
                            K.c_pflagf[:], ALU.mult), deps=gdep("vlastf"))
        strips = []
        i = 0
        while i < J:
            strips.append(list(range(i, min(i + 3, J))))
            i += 3
        qap = VB[:, K.QF_O + 512 * g:K.QF_O + 512 * (g + 1)]
        pv_first = True
        for bi, chunks in enumerate(strips):
            sp = bi % 2
            strip = PSA if sp == 0 else PSB
            prb = K.PRB[sp][:]
            dd = ([last_exp[sp]] if sp in last_exp else [])
            sc = []
            for ci, ch in enumerate(chunks):
                lhs = VB[:, K.KF_O + 128 * ch:K.KF_O + 128 * (ch + 1)]
                sc.append(P.op("t", K.MM(strip[:, 512 * ci:512 * (ci + 1)], lhs, qap,
                                         True, True), deps=dd + [t_bc]))
            w = 512 * len(chunks)
            te = P.op("s", K.ACTV(prb[:, 0:w], strip[:, 0:w], AF.Exp,
                                  bias=K.CMB[:, 16:17], scale=1.0 / float(np.sqrt(FHD))),
                      deps=sc + ([last_pv_rd[sp]] if sp in last_pv_rd else []))
            last_exp[sp] = te
            if any(c2 >= J - 8 for c2 in chunks):
                for ci2, ch2 in enumerate(chunks):
                    if ch2 >= J - 8:
                        jj = ch2 - (J - 8)
                        P.op("v", K.TT(prb[:, 512 * ci2:512 * (ci2 + 1)],
                                       prb[:, 512 * ci2:512 * (ci2 + 1)],
                                       K.c_trail[:, 512 * jj:512 * (jj + 1)],
                                       ALU.mult), deps=[te])
                probs_rdy = ("v", P.cnt["v"])
            else:
                probs_rdy = te
            for ci, ch in enumerate(chunks):
                last4 = ch >= J - 4
                if last4:
                    vap = VB[:, K.VLASTF_O + 128 * (ch % 4):K.VLASTF_O + 128 * (ch % 4) + 128]
                    slhs = K.c_pflagb[:]
                else:
                    vap = VB[:, K.VGF_O + 128 * ch:K.VGF_O + 128 * (ch + 1)]
                    slhs = K.c_onescb[:]
                dd2 = [probs_rdy] + ([vl] if last4 else []) + \
                      ([aot_tok] if (pv_first and aot_tok) else []) + \
                      (gdep("psd") + gdep("pscS") if pv_first else [])
                tpv = P.op("t", K.MM(PSC[:, 0:512], vap, prb[:, 512 * ci:512 * (ci + 1)],
                                     pv_first, ch == J - 1), deps=dd2)
                tsm = P.op("t", K.MM(PSD[0:1, 0:512], slhs, prb[:, 512 * ci:512 * (ci + 1)],
                                     pv_first, ch == J - 1), deps=dd2)
                pv_first = False
                last_pv_rd[sp] = tsm
        pv_t = tpv
        sm_t = tsm
        ts1 = P.op("v", K.CP(K.STG[:], PSD[0:1, 0:512]), deps=[sm_t] + gdep("stg"))
        guard["psd"] = ts1
        ts2 = P.op("v", K.RECIP(K.STG[:], K.STG[:]), deps=[ts1])
        tb = P.op("t", K.MM(PSB[:, 0:512], K.c_onesf[:], K.STG[:], True, True),
                  deps=[ts2])
        guard["stg"] = tb
        tev = P.op("v", K.CP(VB[:, K.AOT_O + 512 * g:K.AOT_O + 512 * (g + 1)],
                             PSC[:, 0:512]), deps=[pv_t])
        guard["pscS"] = tev
        aot_tok = P.op("v", K.TT(VB[:, K.AOT_O + 512 * g:K.AOT_O + 512 * (g + 1)],
                                 VB[:, K.AOT_O + 512 * g:K.AOT_O + 512 * (g + 1)],
                                 PSB[:, 0:512], ALU.mult), deps=[tev, tb])
        last_exp[1] = aot_tok
        guard["vlastf"] = pv_t
    P.barrier()

    if PHASE_LIMIT <= 9:
        return _finish_early(K)
    for g in range(NG):
        rr_toks = []
        for j in range(4):
            t_ = 4 * g + j
            trj = P.op("t", K.TR(PSD[0:16, 0:128], K.RW[:, 16 * t_:16 * (t_ + 1)],
                                 K.c_identf[:]), deps=[rw_toks[t_]] + gdep("psd"))
            cpj = P.op("v", K.CP(K.POOL[0:16, 128 * j:128 * (j + 1)], PSD[0:16, 0:128]),
                       deps=[trj] + gdep("poolrw"))
            guard["psd"] = cpj
            rr_toks.append(cpj)
        aoe_toks = {}
        for e in range(E):
            trow = P.dma("g", K.DMA(K.STG[:], K.POOL[e:e + 1, 0:512]),
                         deps=rr_toks + gdep("stg"), sem="gr")
            bb = P.op("t", K.MM(PSD[:, 0:512], K.c_onesf[:], K.STG[:], True, True),
                      deps=[trow] + gdep("psd"))
            guard["stg"] = bb
            aoe = VB[:, K.AOE_O + 512 * e:K.AOE_O + 512 * (e + 1)]
            aoe_toks[e] = P.op("v", K.TT(aoe, VB[:, K.AOT_O + 512 * g:K.AOT_O + 512 * (g + 1)],
                                         PSD[:, 0:512], ALU.mult),
                               deps=[bb] + gdep("aoe%d" % e))
            guard["psd"] = aoe_toks[e]
        guard["poolrw"] = ("t", P.cnt["t"])
        for co in range(NC):
            wts = []
            wdst = VB[:, K.WOE_O + (co % 2) * 2048:K.WOE_O + (co % 2) * 2048 + 2048]
            for e in range(E):
                wts.append(K.stream_weight(K.oproj[e, :, co * 128:(co + 1) * 128],
                                           wdst[:, 128 * e:128 * (e + 1)],
                                           deps=gdep("woe%d" % (co % 2))))
            ps = PSA[:, 0:512] if co % 2 == 0 else PSA[:, 512:1024]
            mm = None
            for e in range(E):
                mm = P.op("t", K.MM(ps, wdst[:, 128 * e:128 * (e + 1)],
                                    VB[:, K.AOE_O + 512 * e:K.AOE_O + 512 * (e + 1)],
                                    e == 0, e == E - 1),
                          deps=[wts[e], aoe_toks[e]] + gdep("psa%d" % (co % 2)))
            guard["woe%d" % (co % 2)] = mm
            tf = P.op("v", K.STT(K.XU[co][:, g * GW:(g + 1) * GW], ps, 1.0,
                                 K.XU[co][:, g * GW:(g + 1) * GW], ALU.mult, ALU.add),
                      deps=[mm])
            guard["psa%d" % (co % 2)] = tf
            P.dma("g", K.DMA(K.out[co * 128:(co + 1) * 128, g * GW:(g + 1) * GW],
                             K.XU[co][:, g * GW:(g + 1) * GW]), deps=[tf], sem="go")
        for e in range(E):
            guard["aoe%d" % e] = ("t", P.cnt["t"])
    P.barrier()
    K.P.items["g"].append(([("go", P.cnt["go"])], ("NOP",), None, 0))
    return K


def emit(K):
    nc = K.nc
    P = K.P
    import contextlib
    with contextlib.ExitStack() as stack:
        sems = {}
        for name in SEMS:
            sems[name] = stack.enter_context(nc.semaphore("sem_" + name))
        block = stack.enter_context(nc.Block())

        def run_items(eng, items):
            for it in items:
                if isinstance(it, tuple) and len(it) == 3 and it[0] == "SWITCH":
                    _, b0, b1 = it
                    for arm in eng.switch_core_id(n=128):
                        lg = arm.logical
                        if lg is not None and 0 <= lg < NCORES:
                            run_items(eng, b0 if lg % 2 == 0 else b1)
                    continue
                waits, build, sem, inc = it
                for (ps, val) in waits:
                    eng.wait_ge(sems[ps], val)
                if isinstance(build, tuple):
                    if build[0] == "PAD":
                        _, sname, d, pre = build
                        if sname in ("t", "v", "s", "p"):
                            for _i in range(d):
                                eng.nop().then_inc(sems[sname], 1)
                        else:
                            eng.wait_ge(sems[sname], pre)
                            eng.sem_inc(sems[sname], d)
                    continue
                inst = build(eng)
                if sem is not None:
                    inst.then_inc(sems[sem], inc)

        @block.tensor
        def _(eng):
            run_items(eng, P.items["t"])

        @block.vector
        def _(eng):
            run_items(eng, P.items["v"])

        @block.scalar
        def _(eng):
            run_items(eng, P.items["s"])

        @block.sync
        def _(eng):
            run_items(eng, P.items["y"])

        @block.gpsimd
        def _(eng):
            run_items(eng, P.items["g"])

    return nc


def _host_constants():
    c = {}
    bf = ml_dtypes.bfloat16
    c["identb"] = np.eye(128, dtype=bf)
    c["identf"] = np.eye(128, dtype=np.float32)
    ps = np.zeros((128, 128), np.float32)
    for m in range(128):
        k = m + 32 if (m % 64) < 32 else m - 32
        ps[k, m] = 1.0
    c["pswap"] = ps.astype(bf)
    kk = np.arange(128)[:, None]
    qq = np.arange(GW)[None, :]
    dm = np.zeros((128, 5 * GW), np.float32)
    for jj in range(4):
        dm[:, jj * GW:(jj + 1) * GW] = (128 * jj + kk <= qq)
    dm[:, 4 * GW:5 * GW] = 1.0
    c["dmask"] = dm.astype(bf)
    for par in (0, 1):
        tr = np.ones((128, 8 * GW), np.float32)
        for jj in range(4):
            tr[:, (jj + 4 * par) * GW:(jj + 1 + 4 * par) * GW] =                 dm[:, jj * GW:(jj + 1) * GW]
        c["trailm%d" % par] = tr.astype(bf)
    c["trilm"] = (np.arange(128)[:, None] <= np.arange(128)[None, :]).astype(bf)
    c["bandm"] = (np.arange(128)[:, None] > np.arange(128)[None, :]).astype(bf)
    s2 = np.zeros((16, 8 * 128), np.float32)
    for ci in range(8):
        s2[2 * ci, ci * 128:ci * 128 + 64] = 1.0
        s2[2 * ci + 1, ci * 128 + 64:ci * 128 + 128] = 1.0
    c["sel2"] = s2
    sk = np.zeros((4, 2 * 128), np.float32)
    for kc in range(2):
        sk[2 * kc, kc * 128:kc * 128 + 64] = 1.0
        sk[2 * kc + 1, kc * 128 + 64:kc * 128 + 128] = 1.0
    c["selk"] = sk
    s4 = np.zeros((4, 4 * 128), np.float32)
    for tc in range(4):
        s4[tc, tc * 128:(tc + 1) * 128] = 1.0
    c["sel4"] = s4
    tsel = np.zeros((128, 16), np.float32)
    for tc in range(4):
        tsel[:, 4 * tc + tc] = 1.0
    c["tcsel"] = tsel
    hs = np.zeros((128, 8 * 16), np.float32)
    for ci in range(8):
        hs[0:64, 16 * ci + 2 * ci] = 1.0
        hs[64:128, 16 * ci + 2 * ci + 1] = 1.0
    c["hsel"] = hs
    hk = np.zeros((128, 8), np.float32)
    for kc in range(2):
        hk[0:64, 4 * kc + 2 * kc] = 1.0
        hk[64:128, 4 * kc + 2 * kc + 1] = 1.0
    c["hselk"] = hk
    c["onesf"] = np.ones((1, 128), np.float32)
    c["onescf"] = np.ones((128, 1), np.float32)
    c["onescb"] = np.ones((128, 1), bf)
    return c


def _own_positions(p):
    pos = []
    for g in range(NG):
        Sg = 2 * g + p
        pos += list(range(Sg * GW, (Sg + 1) * GW))
    return np.array(pos, dtype=np.int64)


_NC_CACHE = {}


def kernel(L0_attn_hidden, L0_attn_norm1_w, L0_attn_norm2_w,
           L0_attn_g_wq, L0_attn_g_kv_down, L0_attn_g_k_up, L0_attn_g_v_up,
           L0_attn_g_wo, L0_attn_l_wq, L0_attn_l_wk, L0_attn_l_wv, L0_attn_l_wo,
           L0_ffn_sim_matrix, L0_ffn_gates,
           L0_ffn_q_proj, L0_ffn_k_proj, L0_ffn_v_proj, L0_ffn_o_proj):
    f32 = np.float32
    bf = ml_dtypes.bfloat16

    def q_layout(w):
        w = np.asarray(w, f32)
        cols = []
        for (hA, hB) in PAIR:
            for h in (hA, hB):
                blk = w[:, h * HD:(h + 1) * HD]
                blk = np.concatenate([blk[:, 0::2], blk[:, 1::2]], axis=1)
                cols.append(blk)
        return np.ascontiguousarray(np.concatenate(cols, axis=1))

    def wo_layout(w):
        w = np.asarray(w, f32)
        rows = []
        for (hA, hB) in PAIR:
            rows.append(w[hA * HD:(hA + 1) * HD, :])
            rows.append(w[hB * HD:(hB + 1) * HD, :])
        return np.ascontiguousarray(np.concatenate(rows, axis=0))

    wq_g = q_layout(L0_attn_g_wq)
    wq_l = q_layout(L0_attn_l_wq)
    wo_g = wo_layout(L0_attn_g_wo)
    wo_l = wo_layout(L0_attn_l_wo)
    kup = np.ascontiguousarray(_deinterleave_cols(np.asarray(L0_attn_g_k_up, f32), HD))
    wkl = np.ascontiguousarray(_deinterleave_cols(np.asarray(L0_attn_l_wk, f32), HD))
    kvd = np.ascontiguousarray(np.asarray(L0_attn_g_kv_down, f32))
    vup = np.ascontiguousarray(np.asarray(L0_attn_g_v_up, f32))
    wvl = np.ascontiguousarray(np.asarray(L0_attn_l_wv, f32))
    sim = np.ascontiguousarray(np.asarray(L0_ffn_sim_matrix, f32))
    gates = np.ascontiguousarray(np.asarray(L0_ffn_gates, f32).reshape(1, E))
    qkvf = np.ascontiguousarray(np.concatenate(
        [np.asarray(L0_ffn_q_proj, f32), np.asarray(L0_ffn_k_proj, f32),
         np.asarray(L0_ffn_v_proj, f32)], axis=2))
    oproj = np.ascontiguousarray(np.asarray(L0_ffn_o_proj, f32))
    n1w = np.ascontiguousarray(np.asarray(L0_attn_norm1_w, f32).reshape(8, 128).T)
    n2w = np.ascontiguousarray(np.asarray(L0_attn_norm2_w, f32).reshape(8, 128).T)
    consts = _host_constants()
    hidden = np.asarray(L0_attn_hidden, f32)

    if "prog" not in _NC_CACHE:
        _NC_CACHE["prog"] = emit(build_program())
    nc = _NC_CACHE["prog"]

    in_maps = []
    pos_cache = {}
    for core in range(NCORES):
        b, p = core // 2, core % 2
        if p not in pos_cache:
            pos = _own_positions(p)
            cg, sg = _rope_tables(pos.astype(np.float64), 1e6)
            cl, sl = _rope_tables(pos.astype(np.float64), 1e4)
            pos_cache[p] = (pos, np.ascontiguousarray(cg), np.ascontiguousarray(sg),
                            np.ascontiguousarray(cl), np.ascontiguousarray(sl))
        pos, cg, sg, cl, sl = pos_cache[p]
        xs = np.ascontiguousarray(hidden[b][pos].T)
        pf = np.full((128, 1), float(p), f32)
        in_maps.append(dict(
            x=xs, wq_g=wq_g, kvd=kvd, kup=kup, vup=vup, wo_g=wo_g,
            wq_l=wq_l, wk_l=wkl, wv_l=wvl, wo_l=wo_l,
            sim=sim, gates=gates, qkvf=qkvf, oproj=oproj,
            n1w=n1w, n2w=n2w,
            ctab_g=cg, stab_g=sg, ctab_l=cl, stab_l=sl,
            identb=consts["identb"], identf=consts["identf"],
            pswap=consts["pswap"],
            trilm=consts["trilm"], bandm=consts["bandm"],
            sel2=consts["sel2"], selk=consts["selk"], sel4=consts["sel4"],
            tcsel=consts["tcsel"],
            hsel=consts["hsel"], hselk=consts["hselk"],
            onesf=consts["onesf"], onescf=consts["onescf"],
            onescb=consts["onescb"],
            pflagb=pf.astype(bf), pflagf=pf, pflag0f=1.0 - pf,
            trailm=consts["trailm%d" % p],
            cbias=np.tile(np.array(CBIAS_VALS, np.float32), (128, 1)),
        ))

    res = run_bass_kernel_spmd(nc, in_maps, core_ids=list(range(NCORES)))

    out = np.empty((B, S, D), np.float32)
    for core in range(NCORES):
        b, p = core // 2, core % 2
        pos = pos_cache[p][0]
        out[b][pos] = res.results[core]["y"].T
    return out


if __name__ == "__main__":
    K = build_program()
    print("planned:", {k: len(v) for k, v in K.P.items.items()},
          "sems:", K.P.cnt)
    nc = emit(K)
    print("emitted ok")
